# version 01 (final)
# baseline (speedup 1.0000x reference)
import numpy as np
from contextlib import ExitStack

import concourse.bass as bass
import concourse.mybir as mybir
from concourse.bass_utils import run_bass_kernel_spmd

F32 = mybir.dt.float32
BF16 = mybir.dt.bfloat16
AF = mybir.ActivationFunctionType
ALU = mybir.AluOpType

P, DD, G = 128, 8, 8
S, NS, T = 512, 8, 4096
SB, NB = 256, 16
NE = 16
EPS = 1e-5
D0 = 1e-13

ENGINES = ("sync", "tensor", "scalar", "vector", "gpsimd")

C_TMR, C_TMK, C_TMV = 0, 8, 16
C_TD, C_TF = 24, 32
C_LNG, C_LNB = 40, 48
C_GS, C_BS = 56, 64
C_B1S, C_B2S = 72, 80
C_GE = 88
C_BE = 88 + 128
C_CONST = 88 + 256
SM_COLS = C_CONST + 8


class Prog:
    def __init__(self, nc, es):
        self.nc = nc
        self.streams = {e: [] for e in ENGINES}
        self.meta = {e: [] for e in ENGINES}
        self.sem = {}
        self.cnt = {}
        self.es = es

    def new_sem(self, name):
        self.sem[name] = self.es.enter_context(self.nc.semaphore(name))
        self.cnt[name] = 0

    serial = False
    ser_rng = None
    ser_count = 0

    def emit(self, eng, build_fn, waits=(), inc=None, amt=1):
        post = None
        if Prog.serial and (Prog.serial is True or eng in Prog.serial):
            waits = list(waits) + [(k, v) for k, v in self.cnt.items() if v > 0]
        elif Prog.ser_rng and eng == "vector":
            idx = len(self.streams["vector"])
            if Prog.ser_rng[0] <= idx < Prog.ser_rng[1]:
                waits = list(waits) + [(k, v) for k, v in self.cnt.items() if v > 0]
                Prog.ser_count += 1
        if inc is not None:
            self.cnt[inc] += amt
            post = self.cnt[inc]
        semd = self.sem
        wl = tuple((s, v) for (s, v) in waits if v is not None and v > 0)

        def f(e, wl=wl, inc=inc, amt=amt, build_fn=build_fn):
            for s, v in wl:
                e.wait_ge(semd[s], v)
            r = build_fn(e)
            if inc is not None:
                if r is None:
                    r = e.nop()
                r.then_inc(semd[inc], amt)
        if eng == "vector" and Prog.ser_rng:
            idx2 = len(self.streams["vector"])
            if abs(idx2 - Prog.ser_rng[0]) <= 4 or abs(idx2 - Prog.ser_rng[1]) <= 4:
                import sys as _s
                print(f"[dvemap] idx={idx2} line={build_fn.__code__.co_firstlineno}", file=_s.stderr)
        self.streams[eng].append(f)
        self.meta[eng].append((wl, inc, amt))
        return post

    def check_deadlock(self):
        pos = {e: 0 for e in ENGINES}
        cur = {k: 0 for k in self.cnt}
        progressed = True
        while progressed:
            progressed = False
            for e in ENGINES:
                while pos[e] < len(self.meta[e]):
                    wl, inc, amt = self.meta[e][pos[e]]
                    if all(cur[s] >= v for s, v in wl):
                        if inc is not None:
                            cur[inc] += amt
                        pos[e] += 1
                        progressed = True
                    else:
                        break
        stuck = {e: (pos[e], len(self.meta[e])) for e in ENGINES if pos[e] < len(self.meta[e])}
        if stuck:
            detail = {}
            for e in stuck:
                wl, inc, amt = self.meta[e][pos[e]]
                detail[e] = [(s, v, cur[s]) for s, v in wl if cur[s] < v]
            raise RuntimeError(f"DEADLOCK: stuck={stuck} waits={detail} counters={cur}")

    def val(self, sem):
        return self.cnt[sem]


_DEBUG = {}


def build_nc():
    dbg_slabs = _DEBUG.get("slabs", NS)
    dbg_groups = _DEBUG.get("groups", G)
    dbg_groups_late = _DEBUG.get("groups_late", None)
    dbg_parts = _DEBUG.get("parts", 99)
    Prog.ser_rng = _DEBUG.get("ser_rng", None)
    Prog.ser_count = 0
    sser = _DEBUG.get("serial", False)
    Prog.serial = True if sser in (1, True) else (set(sser.split(",")) if isinstance(sser, str) and sser else False)
    dbg_stop = _DEBUG.get("stop", None)
    dbg = dbg_slabs < NS or dbg_stop is not None or dbg_groups < G
    nc = bass.Bass("TRN2", target_bir_lowering=False)

    din = lambda name, shape: nc.dram_tensor(name, shape, F32, kind="ExternalInput").ap()
    xT = din("xT", [1024, T])
    wrT = din("wrT", [1024, 1024]); wkT = din("wkT", [1024, 1024])
    wvT = din("wvT", [1024, 1024]); woT = din("woT", [1024, 1024])
    w1s = din("w1s", [1024, 1024]); w2s = din("w2s", [1024, 1024])
    w1e = din("w1e", [NE, 1024, 1024]); w2e = din("w2e", [NE, 1024, 1024])
    b1e_in = din("b1e", [NE, 1024]); b2e_in = din("b2e", [NE, 1024])
    smallpack = din("smallpack", [P, SM_COLS])
    ones_bc = din("ones_bc", [1, P])
    id16_in = din("id16", [16, 16])
    mask64_in = din("mask64", [P, 2])
    mask64T_in = din("mask64T", [2, P])

    outT = nc.dram_tensor("outT", [1024, T], F32, kind="ExternalOutput").ap()
    stats_dram = nc.dram_tensor("stats_dram", [2, T], F32, kind="ExternalOutput" if dbg else "Internal").ap()
    if dbg:
        dbg_rwkv = nc.dram_tensor("dbg_rwkv", [P, G, T], BF16, kind="ExternalOutput").ap()
        dbg_car = nc.dram_tensor("dbg_car", [P, 4 * G], F32, kind="ExternalOutput").ap()
        dbg_scan = nc.dram_tensor("dbg_scan", [P, 16 * S], F32, kind="ExternalOutput").ap()
    outTr = outT.rearrange("(a p) t -> p a t", p=P)
    xTr = xT.rearrange("(a p) t -> p a t", p=P)

    es = ExitStack()
    sb = lambda n, sh, dt=F32: es.enter_context(nc.sbuf_tensor(n, sh, dt))
    psb = lambda n, sh: es.enter_context(nc.psum_tensor(n, sh, F32))

    rwkv = sb("rwkv", [P, G, T], BF16)
    wres = sb("wres", [P, 3, DD, 1024], BF16)
    small = sb("small", [P, SM_COLS])
    onesbc = sb("onesbc", [1, P])
    id16 = sb("id16sb", [16, 16])
    m64 = sb("m64", [P, 2])
    m64T = sb("m64T", [2, P])
    lam = sb("lam", [P, G])
    wpp = sb("wpp", [P, G])
    acc1 = sb("acc1", [P, G, NS])
    acc2 = sb("acc2", [P, G, NS])
    carD = sb("carD", [P, G])
    carA = sb("carA", [P, G])
    sums = sb("sums", [P, 16])
    hstat = sb("hstat", [2, 16])
    hval = sb("hval", [2, 16])
    scb = sb("scb", [P, 16])
    sc_ch = sb("sc_ch", [P, G])
    bi_ch = sb("bi_ch", [P, G])
    ones_bf = sb("ones_bf", [P, 1], BF16)
    AR_WORDS = 23600
    arena = sb("arena", [P, AR_WORDS])

    class Carver:
        def __init__(self):
            self.off = 0

        def f32(self, shape):
            n = int(np.prod(shape))
            self.off = (self.off + 63) // 64 * 64
            o = self.off // 4
            self.off += n * 4
            assert self.off <= AR_WORDS * 4, "arena overflow"
            v = arena[:, o:o + n]
            return v.rearrange("p (a t) -> p a t", a=shape[0]) if len(shape) == 2 else v

        def b16(self, shape):
            n = int(np.prod(shape))
            self.off = (self.off + 63) // 64 * 64
            o = self.off // 4
            self.off += n * 2
            assert self.off <= AR_WORDS * 4, "arena overflow"
            nw = (n * 2 + 3) // 4
            v = arena[:, o:o + nw].bitcast(BF16)[:, :n]
            return v.rearrange("p (a t) -> p a t", a=shape[0]) if len(shape) == 2 else v

    def row_view(cv, rows, cols):
        n = cols
        cv.off = (cv.off + 63) // 64 * 64
        o = cv.off // 4
        cv.off += n * 4
        assert cv.off <= AR_WORDS * 4, "arena overflow"
        return arena[0:rows, o:o + n]

    cvS = Carver()
    st_m = row_view(cvS, 1, S)
    st_q = row_view(cvS, 1, S)
    r_rstd = row_view(cvS, 1, S)
    r_R = row_view(cvS, 1, S)
    rscr = row_view(cvS, 1, S)
    base_off = cvS.off

    cvA = Carver()
    cvA.off = base_off
    xbufP = cvA.f32([DD, S])
    xsq = cvA.b16([DD, S])
    xn = cvA.b16([DD, S + 1])
    xr_sb = cvA.b16([DD, S])
    xk_sb = cvA.b16([DD, S])
    xv_sb = cvA.b16([DD, S])
    E_b = cvA.f32([S])
    vc_b = cvA.f32([S])
    Dx_b = cvA.f32([S + 4])
    Ax_b = cvA.f32([S + 4])
    bb_b = cvA.f32([S])
    pr1 = cvA.f32([2 * S])
    pr2a = cvA.f32([2 * S])
    scrA = cvA.f32([S])
    scrX = cvA.f32([S])

    cvB = Carver()
    cvB.off = base_off
    mrow_b = row_view(cvB, 1, S)
    Rrow_b = row_view(cvB, 1, S)
    y2e = cvB.f32([DD, NE])
    he_fm = cvB.f32([DD, NE])
    fe_fm = cvB.f32([DD, NE])
    xB = cvB.f32([DD, SB])
    gB = cvB.f32([DD, SB])
    stg = [cvB.f32([DD, S]), cvB.f32([DD, S])]
    y2B = cvB.b16([DD, SB])
    h1B = cvB.b16([DD, SB])
    nrmB = cvB.b16([DD, SB])
    scr1B = cvB.f32([SB])
    scr2B = cvB.f32([SB])
    he_tm = row_view(cvB, 16, 1024)
    fe_tm = row_view(cvB, 16, 1024)

    psK = [psb("psK0", [P, S]), psb("psK1", [P, S])]
    psV = [psb("psV0", [P, S]), psb("psV1", [P, S])]
    psR = psb("psR", [P, S])
    psStat = psb("psStat", [16, S])
    psBCM = psb("psBCM", [P, S])
    psBCR = psb("psBCR", [P, S])

    ge_v = small[:, C_GE:C_GE + 128].rearrange("p (a n) -> p a n", a=DD)
    be_v = small[:, C_BE:C_BE + 128].rearrange("p (a n) -> p a n", a=DD)
    c_recipD = small[:, C_CONST + 0:C_CONST + 1]
    c_eps = small[:, C_CONST + 4:C_CONST + 5]
    c_mhalf = small[:, C_CONST + 5:C_CONST + 6]

    es2 = ExitStack()
    pr = Prog(nc, es2)
    for sname in ["qs", "qe", "sPE", "sACT", "sDVE", "sPOOL"]:
        pr.new_sem(sname)
    blk = es2.enter_context(nc.Block())
    W = lambda *a: [x for x in a]

    for ap_, src in [(small[:], smallpack), (onesbc[:], ones_bc), (id16[:], id16_in),
                     (m64[:], mask64_in), (m64T[:], mask64T_in)]:
        pr.emit("sync", lambda e, ap_=ap_, src=src: e.dma_start(out=ap_, in_=src), inc="qs", amt=16)
    smalls_ld = pr.val("qs")

    pr.emit("scalar", lambda e: e.activation(out=wpp[:], in_=small[:, C_TD:C_TD + 8], func=AF.Exp),
            waits=W(("qs", smalls_ld)), inc="sACT")
    pr.emit("vector", lambda e: e.tensor_scalar(out=wpp[:], in0=wpp[:], scalar1=-1.0, scalar2=None, op0=ALU.mult),
            waits=W(("sACT", pr.val("sACT"))), inc="sDVE")
    pr.emit("scalar", lambda e: e.activation(out=lam[:], in_=wpp[:], func=AF.Exp),
            waits=W(("sDVE", pr.val("sDVE"))), inc="sACT")
    lam_act = pr.val("sACT")
    pr.emit("vector", lambda e: e.tensor_scalar(
        out=carD[:], in0=small[:, C_CONST + 1:C_CONST + 2].broadcast_to((P, G)),
        scalar1=1.0, scalar2=None, op0=ALU.mult), waits=W(("qs", smalls_ld)), inc="sDVE")
    pr.emit("vector", lambda e: e.tensor_scalar(
        out=carA[:], in0=small[:, C_CONST + 2:C_CONST + 3].broadcast_to((P, G)),
        scalar1=1.0, scalar2=None, op0=ALU.mult), inc="sDVE")
    pr.emit("vector", lambda e: e.tensor_scalar(
        out=xn[:, :, 0:1].rearrange("p a t -> p (a t)"),
        in0=small[:, C_CONST + 2:C_CONST + 3].broadcast_to((P, DD)),
        scalar1=1.0, scalar2=None, op0=ALU.mult), inc="sDVE")
    pr.emit("vector", lambda e: e.tensor_copy(out=ones_bf[:], in_=small[:, C_CONST + 0:C_CONST + 1]),
            waits=W(("qs", smalls_ld)), inc="sDVE")
    init_dve = pr.val("sDVE")

    if dbg:
        pr.emit("sync", lambda e: e.dma_start(out=dbg_scan[:, 15 * S:15 * S + 8], in_=E_b[:, 0:8]),
                waits=W(("sDVE", init_dve), ("sACT", lam_act)), inc="qs", amt=16)
        pr.emit("sync", lambda e: e.dma_start(out=dbg_scan[:, 15 * S + 8:15 * S + 16], in_=wpp[:, :]),
                inc="qs", amt=16)

    last_cast = None
    for mi, wsrc in enumerate([wrT, wkT, wvT]):
        for half in range(2):
            srcw = wsrc.rearrange("(a p) o -> p a o", p=P)[:, :, half * S:(half + 1) * S]
            wv_ = [] if last_cast is None else W(("sDVE", last_cast))
            dma_v = pr.emit("sync", lambda e, srcw=srcw: e.dma_start(out=xbufP[:, :, :], in_=srcw),
                            waits=wv_, inc="qs", amt=16)
            last_cast = pr.emit("vector", lambda e, mi=mi, half=half: e.tensor_copy(
                out=wres[:, mi, :, half * S:(half + 1) * S], in_=xbufP[:, :, :]),
                waits=W(("qs", dma_v)), inc="sDVE")
    wprep_done = last_cast

    x_ready = [None] * NS
    statmm_done = [None] * NS
    stat_evac = [None] * NS
    statrow_done = [None] * NS
    statdma_done = [None] * NS
    bc_done = [None] * NS
    xn_done = [None] * NS
    mix_done = [None] * NS
    NSG = NS * G
    t_k = [None] * NSG; t_v = [None] * NSG; t_r = [None] * NSG
    t_evE = [None] * NSG; t_evV = [None] * NSG; t_evR = [None] * NSG
    t_bb = [None] * NSG; t_z0 = [None] * NSG; t_zts = [None] * NSG
    t_ab = [None] * NSG; t_bv = [None] * NSG; t_scanA = [None] * NSG
    t_sig = [None] * NSG; t_d = [None] * NSG; t_od = [None] * NSG
    t_radd = [None] * NSG; t_rwkv = [None] * NSG; t_sq2 = [None] * NSG

    xof = lambda s: xbufP

    x_ready[0] = pr.emit("sync", lambda e: e.dma_start(out=xof(0)[:, :, :], in_=xTr[:, :, 0:S]),
                         waits=W(("sDVE", wprep_done)), inc="qs", amt=16)

    for s in range(dbg_slabs):
        xb, sq = xof(s), xsq
        t0 = s * S
        i0 = s * G

        sq_v = None
        for dd in range(DD):
            wv_ = []
            if dd == 0:
                wv_ = W(("qs", x_ready[s]))
                if s > 0:
                    wv_ += W(("sPE", statmm_done[s - 1]))
            sq_v = pr.emit("scalar", lambda e, dd=dd, xb=xb, sq=sq: e.activation(
                out=sq[:, dd, :], in_=xb[:, dd, :], func=AF.Square), waits=wv_, inc="sACT")

        mm1_done = None
        for dd in range(DD):
            wv_ = []
            if dd == 0:
                wv_ = W(("qs", x_ready[s]), ("qs", smalls_ld))
                if s > 0:
                    wv_ += W(("sACT", stat_evac[s - 1]))
            mm1_done = pr.emit("tensor", lambda e, dd=dd, xb=xb: e.matmul(
                psStat[0:1, :], lhsT=c_recipD, rhs=xb[:, dd, :],
                start=(dd == 0), stop=(dd == DD - 1), skip_group_check=True),
                waits=wv_, inc="sPE" if dd == DD - 1 else None)
        evac_m = pr.emit("scalar", lambda e: e.activation(out=st_m[:], in_=psStat[0:1, :], func=AF.Copy),
                         waits=W(("sPE", mm1_done)), inc="sACT")
        for dd in range(DD):
            wv_ = []
            if dd == 0:
                wv_ = W(("sACT", sq_v), ("sACT", evac_m))
            statmm_done[s] = pr.emit("tensor", lambda e, dd=dd, sq=sq: e.matmul(
                psStat[0:1, :], lhsT=ones_bf[:], rhs=sq[:, dd, :],
                start=(dd == 0), stop=(dd == DD - 1), skip_group_check=True),
                waits=wv_, inc="sPE" if dd == DD - 1 else None)

        wv_ = W(("sPE", statmm_done[s]))
        if s > 0:
            wv_ += W(("sDVE", statrow_done[s - 1]), ("sPE", bc_done[s - 1]), ("qs", statdma_done[s - 1]))
        stat_evac[s] = pr.emit("scalar", lambda e: e.activation(out=st_q[:], in_=psStat[0:1, :], func=AF.Copy),
                               waits=wv_, inc="sACT")
        pr.emit("vector", lambda e: e.tensor_tensor(out=rscr[:], in0=st_m[:], in1=st_m[:], op=ALU.mult),
                waits=W(("sACT", stat_evac[s])), inc="sDVE")
        pr.emit("vector", lambda e: e.tensor_tensor(out=st_q[:], in0=st_q[:], in1=rscr[:], op=ALU.subtract),
                inc="sDVE")
        var_v = pr.val("sDVE")
        pr.emit("scalar", lambda e: e.activation(out=r_rstd[:], in_=st_q[:], func=AF.Ln, bias=c_eps[0:1, :]),
                waits=W(("sDVE", var_v)), inc="sACT")
        pr.emit("scalar", lambda e: e.activation(out=r_rstd[:], in_=r_rstd[:], func=AF.Exp, scale=c_mhalf[0:1, :]),
                inc="sACT")
        rstd_v = pr.val("sACT")
        pr.emit("vector", lambda e: e.tensor_tensor(out=rscr[:], in0=st_q[:], in1=r_rstd[:], op=ALU.mult),
                waits=W(("sACT", rstd_v)), inc="sDVE")
        pr.emit("vector", lambda e: e.tensor_tensor(out=rscr[:], in0=rscr[:], in1=r_rstd[:], op=ALU.mult),
                inc="sDVE")
        v2_v = pr.val("sDVE")
        pr.emit("scalar", lambda e: e.activation(out=r_R[:], in_=rscr[:], func=AF.Ln, bias=c_eps[0:1, :]),
                waits=W(("sDVE", v2_v)), inc="sACT")
        pr.emit("scalar", lambda e: e.activation(out=r_R[:], in_=r_R[:], func=AF.Exp, scale=c_mhalf[0:1, :]),
                inc="sACT")
        rstd2_v = pr.val("sACT")
        statrow_done[s] = pr.emit("vector", lambda e: e.tensor_tensor(
            out=r_R[:], in0=r_R[:], in1=r_rstd[:], op=ALU.mult),
            waits=W(("sACT", rstd2_v)), inc="sDVE")
        pr.emit("sync", lambda e, t0=t0: e.dma_start(out=stats_dram[0:1, t0:t0 + S], in_=st_m[:]),
                waits=W(("sDVE", statrow_done[s])), inc="qs", amt=16)
        statdma_done[s] = pr.emit("sync", lambda e, t0=t0: e.dma_start(out=stats_dram[1:2, t0:t0 + S], in_=r_R[:]),
                                  inc="qs", amt=16)

        bc_w = W(("sDVE", statrow_done[s]))
        if s > 0:
            bc_w += W(("sDVE", xn_done[s - 1]))
        pr.emit("tensor", lambda e: e.matmul(psBCM[:], lhsT=onesbc[:], rhs=st_m[:], start=True, stop=True),
                waits=bc_w, inc="sPE")
        bc_done[s] = pr.emit("tensor", lambda e: e.matmul(psBCR[:], lhsT=onesbc[:], rhs=r_rstd[:], start=True, stop=True),
                             inc="sPE")

        if s > 0:
            pr.emit("vector", lambda e: e.tensor_copy(out=xn[:, :, 0:1], in_=xn[:, :, S:S + 1]),
                    waits=W(("sDVE", mix_done[s - 1])), inc="sDVE")
        for dd in range(DD):
            wv_ = ()
            if dd == 0:
                wv_ = W(("sPE", bc_done[s]))
                if s == 0:
                    wv_ += W(("sDVE", init_dve))
            pr.emit("vector", lambda e, dd=dd, xb=xb: e.tensor_tensor(
                out=scrX[:, :], in0=xb[:, dd, :], in1=psBCM[:], op=ALU.subtract), waits=wv_, inc="sDVE")
            pr.emit("vector", lambda e, dd=dd: e.tensor_tensor(
                out=xn[:, dd, 1:S + 1], in0=scrX[:, :], in1=psBCR[:], op=ALU.mult), inc="sDVE")
        xn_done[s] = pr.val("sDVE")
        if s + 1 < NS:
            x_ready[s + 1] = pr.emit("sync", lambda e, s=s: e.dma_start(
                out=xbufP[:, :, :], in_=xTr[:, :, (s + 1) * S:(s + 2) * S]),
                waits=W(("sDVE", xn_done[s]), ("sPE", statmm_done[s])), inc="qs", amt=16)

        mix_w0 = W(("sPE", t_r[i0 - 1])) if s > 0 else []
        for dd in range(DD):
            shv = xn[:, dd, 0:S]
            xnv = xn[:, dd, 1:S + 1]
            pr.emit("vector", lambda e, shv=shv, xnv=xnv: e.tensor_tensor(
                out=scrX[:, :], in0=xnv, in1=shv, op=ALU.subtract),
                waits=mix_w0 if dd == 0 else (), inc="sDVE")
            for tmc, dst in ((C_TMR, xr_sb), (C_TMK, xk_sb), (C_TMV, xv_sb)):
                pr.emit("vector", lambda e, dd=dd, tmc=tmc, dst=dst, shv=shv: e.scalar_tensor_tensor(
                    out=dst[:, dd, :], in0=scrX[:, :], scalar=small[:, tmc + dd:tmc + dd + 1],
                    in1=shv, op0=ALU.mult, op1=ALU.add), inc="sDVE")
        mix_done[s] = pr.val("sDVE")

        ng_this = dbg_groups if (dbg_groups_late is None or s == 0) else dbg_groups_late
        for g in range(ng_this):
            i = i0 + g
            kps, vps = psK[g % 2], psV[g % 2]
            pair2 = pr2a
            scr = scrA

            wv_ = W(("sDVE", mix_done[s]))
            if i >= 2:
                wv_ += W(("sDVE", t_z0[i - 2]))
            for dd in range(DD):
                t_k[i] = pr.emit("tensor", lambda e, dd=dd, g=g, kps=kps: e.matmul(
                    kps[:], lhsT=wres[:, 1, dd, g * P:(g + 1) * P], rhs=xk_sb[:, dd, :],
                    start=(dd == 0), stop=(dd == DD - 1)),
                    waits=wv_ if dd == 0 else (), inc="sPE" if dd == DD - 1 else None)
            wv_ = W(("sACT", t_evV[i - 2])) if i >= 2 else []
            for dd in range(DD):
                t_v[i] = pr.emit("tensor", lambda e, dd=dd, g=g, vps=vps: e.matmul(
                    vps[:], lhsT=wres[:, 2, dd, g * P:(g + 1) * P], rhs=xv_sb[:, dd, :],
                    start=(dd == 0), stop=(dd == DD - 1)),
                    waits=wv_ if dd == 0 else (), inc="sPE" if dd == DD - 1 else None)
            wv_ = W(("sACT", t_evR[i - 1])) if i >= 1 else []
            for dd in range(DD):
                t_r[i] = pr.emit("tensor", lambda e, dd=dd, g=g: e.matmul(
                    psR[:], lhsT=wres[:, 0, dd, g * P:(g + 1) * P], rhs=xr_sb[:, dd, :],
                    start=(dd == 0), stop=(dd == DD - 1)),
                    waits=wv_ if dd == 0 else (), inc="sPE" if dd == DD - 1 else None)

            if s > 0 and dbg_parts < 1:
                continue
            wv_ = W(("sPE", t_k[i]))
            if i >= 1:
                wv_ += W(("sDVE", t_zts[i - 1]))
            t_evE[i] = pr.emit("scalar", lambda e, kps=kps: e.activation(out=E_b[:, :], in_=kps[:], func=AF.Exp),
                               waits=wv_, inc="sACT")
            wv_ = W(("sPE", t_v[i]))
            if i >= 1:
                wv_ += W(("sDVE", t_radd[i - 1]), ("sDVE", t_bv[i - 1]))
            t_evV[i] = pr.emit("scalar", lambda e, vps=vps: e.activation(out=vc_b[:, :], in_=vps[:], func=AF.Copy),
                               waits=wv_, inc="sACT")
            wv_ = W(("sPE", t_r[i]))
            if i >= 1:
                wv_ += W(("sDVE", t_rwkv[i - 1]), ("sDVE", t_od[i - 1]))
            t_evR[i] = pr.emit("scalar", lambda e: e.activation(out=pr1[:, 0:S], in_=psR[:], func=AF.Exp, scale=-1.0),
                               waits=wv_, inc="sACT")

            if s > 0 and dbg_parts < 2:
                continue
            wv_ = W(("sACT", t_evE[i]), ("sACT", lam_act))
            if i >= 1:
                wv_ += W(("sACT", t_bb[i - 1]))
            else:
                wv_ += W(("sDVE", init_dve))
            if dbg and s == 0 and g == 0:
                snapE0_v = pr.emit("sync", lambda e: e.dma_start(out=dbg_scan[:, 8 * S + 64:9 * S + 64], in_=E_b[:, :]),
                        waits=W(("sACT", t_evE[i])), inc="qs", amt=16)
                pr.emit("sync", lambda e: e.dma_start(out=dbg_scan[:, 13 * S + 192:14 * S + 192],
                        in_=xk_sb.rearrange("p a t -> p (a t)")[:, 0:2 * S].bitcast(F32)),
                        inc="qs", amt=16)
                pr.emit("sync", lambda e: e.dma_start(out=dbg_scan[:, 14 * S + 192:15 * S + 192],
                        in_=xn.rearrange("p a t -> p (a t)")[:, 0:2 * S].bitcast(F32)),
                        inc="qs", amt=16)
                snapE0_v = pr.val("qs")
                wv_ = wv_ + W(("qs", snapE0_v))
            pr.emit("vector", lambda e, g=g: e.tensor_copy(out=Dx_b[:, 0:1], in_=carD[:, g:g + 1]),
                    waits=wv_, inc="sDVE")
            if s > 0 and dbg_parts < 22:
                continue
            scanDop_v = pr.emit("vector", lambda e, g=g: e.tensor_tensor_scan(
                out=Dx_b[:, 1:S + 1], data0=lam[:, g:g + 1].broadcast_to((P, S)), data1=E_b[:, :],
                initial=carD[:, g:g + 1], op0=ALU.mult, op1=ALU.add), inc="sDVE")
            if s > 0 and dbg_parts < 23:
                continue
            pr.emit("vector", lambda e, g=g: e.tensor_copy(out=carD[:, g:g + 1], in_=Dx_b[:, S:S + 1]),
                    waits=W(("sDVE", scanDop_v)), inc="sDVE")
            scanD_v = pr.val("sDVE")
            if dbg and s == 0 and g == 0:
                pr.emit("sync", lambda e: e.dma_start(out=dbg_scan[:, 9 * S:10 * S], in_=E_b[:, :]),
                        waits=W(("sDVE", scanD_v)), inc="qs", amt=16)
                snapE_v = pr.emit("sync", lambda e: e.dma_start(out=dbg_scan[:, 10 * S:11 * S + 4], in_=Dx_b[:, 0:S + 4]),
                        inc="qs", amt=16)

            if s > 0 and dbg_parts < 24:
                continue
            wv_ = W(("sDVE", scanD_v))
            if i >= 1:
                wv_ += W(("sDVE", t_scanA[i - 1]))
            t_bb[i] = pr.emit("scalar", lambda e: e.activation(out=bb_b[:, :], in_=Dx_b[:, 0:S], func=AF.Ln),
                              waits=wv_, inc="sACT")

            if s > 0 and dbg_parts < 30:
                continue
            z0w = W(("sACT", t_bb[i]))
            if dbg and s == 0 and g == 0:
                snapb_v = pr.emit("sync", lambda e: e.dma_start(out=dbg_scan[:, 11 * S + 64:12 * S + 64], in_=bb_b[:, :]),
                        waits=W(("sACT", t_bb[i]), ("qs", snapE_v)), inc="qs", amt=16)
                z0w = W(("sACT", t_bb[i]), ("qs", snapb_v))
            t_z0[i] = pr.emit("vector", lambda e, kps=kps: e.tensor_tensor(
                out=E_b[:, :], in0=bb_b[:, :], in1=kps[:], op=ALU.subtract),
                waits=z0w, inc="sDVE")

            wv_ = W(("sACT", t_sig[i - 1])) if i >= 1 else []
            pr.emit("vector", lambda e, g=g, pair2=pair2: e.tensor_scalar(
                out=pair2[:, 0:S], in0=E_b[:, :], scalar1=wpp[:, g:g + 1], scalar2=0.0, op0=ALU.add, op1=ALU.min),
                waits=wv_, inc="sDVE")
            t_zts[i] = pr.emit("vector", lambda e, g=g, pair2=pair2: e.tensor_scalar(
                out=pair2[:, S:2 * S], in0=E_b[:, :], scalar1=wpp[:, g:g + 1], scalar2=0.0, op0=ALU.add, op1=ALU.max),
                inc="sDVE")
            pr.emit("scalar", lambda e, pair2=pair2: e.activation(out=pair2[:, 0:S], in_=pair2[:, 0:S], func=AF.Exp),
                    waits=W(("sDVE", t_zts[i])), inc="sACT")
            t_ab[i] = pr.emit("scalar", lambda e, pair2=pair2: e.activation(
                out=pair2[:, S:2 * S], in_=pair2[:, S:2 * S], func=AF.Exp, scale=-1.0), inc="sACT")

            if s > 0 and dbg_parts < 40:
                continue
            t_bv[i] = pr.emit("vector", lambda e, pair2=pair2: e.tensor_tensor(
                out=bb_b[:, :], in0=pair2[:, S:2 * S], in1=vc_b[:, :], op=ALU.mult),
                waits=W(("sACT", t_ab[i]), ("sACT", t_evV[i])), inc="sDVE")

            wv_ = W(("sDVE", t_d[i - 1])) if i >= 1 else []
            pr.emit("vector", lambda e, g=g: e.tensor_copy(out=Ax_b[:, 0:1], in_=carA[:, g:g + 1]),
                    waits=wv_, inc="sDVE")
            scanAop_v = pr.emit("vector", lambda e, g=g, pair2=pair2: e.tensor_tensor_scan(
                out=Ax_b[:, 1:S + 1], data0=pair2[:, 0:S], data1=bb_b[:, :],
                initial=carA[:, g:g + 1], op0=ALU.mult, op1=ALU.add), inc="sDVE")
            pr.emit("vector", lambda e, g=g: e.tensor_copy(out=carA[:, g:g + 1], in_=Ax_b[:, S:S + 1]),
                    waits=W(("sDVE", scanAop_v)), inc="sDVE")
            t_scanA[i] = pr.val("sDVE")

            pr.emit("scalar", lambda e, g=g: e.activation(
                out=pr1[:, S:2 * S], in_=E_b[:, :], func=AF.Exp, scale=-1.0,
                bias=small[:, C_TF + g:C_TF + g + 1]),
                waits=W(("sDVE", t_z0[i])), inc="sACT")
            ln1p_w = W(("sDVE", t_scanA[i]))
            if dbg and s == 0 and g == 0:
                snapa_v = pr.emit("sync", lambda e, pair2=pair2: e.dma_start(out=dbg_scan[:, 12 * S + 128:13 * S + 128], in_=pair2[:, 0:S]),
                        waits=W(("sACT", t_ab[i]), ("qs", snapb_v)), inc="qs", amt=16)
                ln1p_w = W(("sDVE", t_scanA[i]), ("qs", snapa_v))
            pr.emit("scalar", lambda e, pair2=pair2: e.activation(
                out=pair2[:, 0:2 * S], in_=pr1[:, 0:2 * S], func=AF.Ln, bias=1.0),
                waits=ln1p_w, inc="sACT")
            t_sig[i] = pr.emit("scalar", lambda e, pair2=pair2: e.activation(
                out=pr1[:, 0:2 * S], in_=pair2[:, 0:2 * S], func=AF.Exp, scale=-1.0), inc="sACT")

            if s > 0 and dbg_parts < 50:
                continue
            wv_ = W(("sDVE", t_scanA[i]), ("sACT", t_evV[i]))
            if i >= 1:
                wv_ += W(("sDVE", t_rwkv[i - 1]), ("sACT", t_sq2[i - 1]))
            POOL_ENG = "vector"
            POOL_SEM = "sDVE"
            if dbg and s == 0 and g == 0:
                pr.emit("sync", lambda e: e.dma_start(out=dbg_scan[:, 2 * S + 32:3 * S + 36], in_=Ax_b[:, 0:S + 4]),
                        waits=W(("sDVE", t_scanA[i])), inc="qs", amt=16)
                pr.emit("sync", lambda e: e.dma_start(out=dbg_scan[:, 3 * S + 64:4 * S + 64], in_=vc_b[:, :]),
                        waits=W(("sACT", t_evV[i])), inc="qs", amt=16)
                snap_sig = pr.emit("sync", lambda e: e.dma_start(out=dbg_scan[:, 4 * S + 96:5 * S + 96], in_=pr1[:, S:2 * S]),
                        waits=W(("sACT", t_sig[i])), inc="qs", amt=16)
                wv_ = wv_ + W(("qs", snap_sig))
            t_d[i] = pr.emit(POOL_ENG, lambda e, scr=scr: e.tensor_tensor(
                out=scr[:, :], in0=Ax_b[:, 0:S], in1=vc_b[:, :], op=ALU.subtract), waits=wv_, inc=POOL_SEM)
            t_od[i] = pr.emit(POOL_ENG, lambda e, scr=scr: e.tensor_tensor(
                out=scr[:, :], in0=scr[:, :], in1=pr1[:, S:2 * S], op=ALU.mult),
                waits=W(("sACT", t_sig[i])), inc=POOL_SEM)
            t_radd[i] = pr.emit(POOL_ENG, lambda e, scr=scr: e.tensor_tensor(
                out=scr[:, :], in0=vc_b[:, :], in1=scr[:, :], op=ALU.add), inc=POOL_SEM)

            rwkv_w = W(("sDVE", t_radd[i]))
            if dbg and s == 0 and g == 0:
                pr.emit("sync", lambda e, scr=scr: e.dma_start(out=dbg_scan[:, 8 * S + 64:9 * S + 64], in_=scr[:, :]),
                        waits=W(("sDVE", t_radd[i])), inc="qs", amt=16)
                snap_sr = pr.emit("sync", lambda e: e.dma_start(out=dbg_scan[:, 9 * S + 128:10 * S + 128], in_=pr1[:, 0:S]),
                        inc="qs", amt=16)
                rwkv_w = rwkv_w + W(("qs", snap_sr))
            t_rwkv[i] = pr.emit("vector", lambda e, s=s, g=g, scr=scr: e.scalar_tensor_tensor(
                out=rwkv[:, g, s * S:(s + 1) * S], in0=scr[:, :], scalar=1.0, in1=pr1[:, 0:S],
                op0=ALU.mult, op1=ALU.mult, accum_out=acc1[:, g, s:s + 1]),
                waits=rwkv_w, inc="sDVE")
            t_sq2[i] = pr.emit("scalar", lambda e, s=s, g=g, scr=scr: e.activation(
                out=scr[:, :], in_=rwkv[:, g, s * S:(s + 1) * S], func=AF.Square, accum_out=acc2[:, g, s:s + 1]),
                waits=W(("sDVE", t_rwkv[i])), inc="sACT")
            if dbg and s == 0 and g == 0:
                pr.emit("sync", lambda e: e.dma_start(
                    out=dbg_scan[:, 15 * S + 16:16 * S].bitcast(BF16),
                    in_=rwkv[:, 0, 0:2 * (S - 16)]),
                    waits=W(("sDVE", t_rwkv[i])), inc="qs", amt=16)

    def dbg_finish():
        lastA2, lastD2, lastP2, lastPE2 = pr.val("sACT"), pr.val("sDVE"), pr.val("sPOOL"), pr.val("sPE")
        pr.emit("gpsimd", lambda e: e.dma_start(out=dbg_rwkv, in_=rwkv[:]),
                waits=W(("sACT", lastA2), ("sDVE", lastD2), ("sPOOL", lastP2), ("sPE", lastPE2)), inc="qs", amt=16)
        pr.emit("gpsimd", lambda e: e.dma_start(out=dbg_car[:, 0:G], in_=carD[:]), inc="qs", amt=16)
        pr.emit("gpsimd", lambda e: e.dma_start(out=dbg_car[:, G:2 * G], in_=carA[:]), inc="qs", amt=16)
        pr.emit("gpsimd", lambda e: e.dma_start(out=dbg_car[:, 2 * G:3 * G], in_=sc_ch[:]), inc="qs", amt=16)
        pr.emit("gpsimd", lambda e: e.dma_start(out=dbg_car[:, 3 * G:4 * G], in_=bi_ch[:]), inc="qs", amt=16)
        pr.emit("gpsimd", lambda e: e.dma_start(out=dbg_scan[:, 0:S + 4], in_=Dx_b[:, :]), inc="qs", amt=16)
        pr.emit("gpsimd", lambda e: e.dma_start(out=dbg_scan[:, S + 4:2 * S + 8], in_=Ax_b[:, :]), inc="qs", amt=16)
        pr.emit("gpsimd", lambda e: e.dma_start(out=dbg_scan[:, 2 * S + 8:4 * S + 8], in_=pr1[:, :]), inc="qs", amt=16)
        pr.emit("gpsimd", lambda e: e.dma_start(out=dbg_scan[:, 4 * S + 8:5 * S + 8], in_=E_b[:, :]), inc="qs", amt=16)
        pr.emit("gpsimd", lambda e: e.dma_start(out=dbg_scan[:, 5 * S + 8:6 * S + 8], in_=bb_b[:, :]), inc="qs", amt=16)
        pr.emit("gpsimd", lambda e: e.dma_start(out=dbg_scan[:, 6 * S + 8:6 * S + 72], in_=acc1.rearrange("p a t -> p (a t)")), inc="qs", amt=16)
        pr.emit("gpsimd", lambda e: e.dma_start(out=dbg_scan[:, 6 * S + 72:6 * S + 136], in_=acc2.rearrange("p a t -> p (a t)")), inc="qs", amt=16)
        pr.emit("gpsimd", lambda e: e.dma_start(out=dbg_scan[:, 6 * S + 136:6 * S + 152], in_=sums[:, :]), inc="qs", amt=16)
        pr.emit("gpsimd", lambda e: e.dma_start(out=dbg_scan[0:2, 6 * S + 160:6 * S + 176], in_=hstat[:, :]), inc="qs", amt=16)
        pr.emit("gpsimd", lambda e: e.dma_start(out=dbg_scan[0:2, 6 * S + 176:6 * S + 192], in_=hval[:, :]), inc="qs", amt=16)
        pr.emit("gpsimd", lambda e: e.dma_start(out=dbg_scan[:, 6 * S + 192:6 * S + 208], in_=scb[:, :]), inc="qs", amt=16)
        pr.emit("gpsimd", lambda e: None, waits=W(("qs", pr.val("qs")), ("qe", pr.val("qe"))))

    if dbg_slabs < NS:
        dbg_finish()
        return _finish_build(nc, pr, blk, es, es2)

    lastA, lastD = pr.val("sACT"), pr.val("sDVE")
    for g in range(G):
        pr.emit("vector", lambda e, g=g: e.reduce_sum(out=sums[:, g:g + 1], in_=acc1[:, g, :], axis=mybir.AxisListType.X),
                waits=W(("sACT", lastA), ("sDVE", lastD)) if g == 0 else (), inc="sDVE")
        pr.emit("vector", lambda e, g=g: e.reduce_sum(out=sums[:, 8 + g:9 + g], in_=acc2[:, g, :], axis=mybir.AxisListType.X),
                inc="sDVE")
    sums_v = pr.val("sDVE")
    pr.emit("tensor", lambda e: e.matmul(psStat[0:2, 0:16], lhsT=m64[:], rhs=sums[:, 0:16], start=True, stop=True),
            waits=W(("sDVE", sums_v), ("sACT", stat_evac[NS - 1])), inc="sPE")
    hc_v = pr.val("sPE")
    pr.emit("scalar", lambda e: e.activation(out=hstat[:], in_=psStat[0:2, 0:16], func=AF.Copy),
            waits=W(("sPE", hc_v)), inc="sACT")
    hev = pr.val("sACT")
    NTOT = float(64 * T)
    pr.emit("vector", lambda e: e.tensor_scalar(out=hstat[:, 0:16], in0=hstat[:, 0:16],
                                                scalar1=1.0 / NTOT, scalar2=None, op0=ALU.mult),
            waits=W(("sACT", hev)), inc="sDVE")
    pr.emit("vector", lambda e: e.tensor_tensor(out=hval[:, 0:8], in0=hstat[:, 0:8], in1=hstat[:, 0:8], op=ALU.mult),
            waits=W(("sDVE", pr.val("sDVE"))), inc="sDVE")
    pr.emit("vector", lambda e: e.tensor_tensor(out=hstat[:, 8:16], in0=hstat[:, 8:16], in1=hval[:, 0:8], op=ALU.subtract),
            waits=W(("sDVE", pr.val("sDVE"))), inc="sDVE")
    hv_v = pr.val("sDVE")
    pr.emit("scalar", lambda e: e.activation(out=hstat[:, 8:16], in_=hstat[:, 8:16], func=AF.Ln, bias=c_eps[0:2, :]),
            waits=W(("sDVE", hv_v)), inc="sACT")
    pr.emit("scalar", lambda e: e.activation(out=hstat[:, 8:16], in_=hstat[:, 8:16], func=AF.Exp, scale=c_mhalf[0:2, :]),
            inc="sACT")
    hr_v = pr.val("sACT")
    pr.emit("vector", lambda e: e.tensor_copy(out=hval[:, 0:16], in_=hstat[:, 0:16]),
            waits=W(("sACT", hr_v)), inc="sDVE")
    hcp_v = pr.val("sDVE")
    pr.emit("tensor", lambda e: e.matmul(psBCM[:, 0:16], lhsT=m64T[:], rhs=hval[:, 0:16], start=True, stop=True),
            waits=W(("sDVE", hcp_v), ("sDVE", xn_done[NS - 1])), inc="sPE")
    dist_v = pr.val("sPE")
    pr.emit("scalar", lambda e: e.activation(out=scb[:], in_=psBCM[:, 0:16], func=AF.Copy),
            waits=W(("sPE", dist_v)), inc="sACT")
    scb_v = pr.val("sACT")
    pr.emit("vector", lambda e: e.tensor_tensor(out=sc_ch[:], in0=scb[:, 8:16], in1=small[:, C_LNG:C_LNG + 8], op=ALU.mult),
            waits=W(("sACT", scb_v)), inc="sDVE")
    pr.emit("vector", lambda e: e.tensor_tensor(out=bi_ch[:], in0=scb[:, 0:8], in1=sc_ch[:], op=ALU.mult),
            waits=W(("sDVE", pr.val("sDVE"))), inc="sDVE")
    pr.emit("vector", lambda e: e.tensor_tensor(out=bi_ch[:], in0=small[:, C_LNB:C_LNB + 8], in1=bi_ch[:], op=ALU.subtract),
            waits=W(("sDVE", pr.val("sDVE"))), inc="sDVE")
    gn_v = pr.val("sDVE")

    if dbg_stop == "A":
        dbg_finish()
        return _finish_build(nc, pr, blk, es, es2)

    last_cast = None
    ilast = NSG - 1
    for mi, wsrc in enumerate([woT, w1s, w2s]):
        for half in range(2):
            srcw = wsrc.rearrange("(a p) o -> p a o", p=P)[:, :, half * S:(half + 1) * S]
            wv_ = []
            if mi == 0 and half == 0:
                wv_ = W(("sPE", t_r[ilast]), ("sDVE", xn_done[NS - 1]), ("sPE", statmm_done[NS - 1]))
            if last_cast is not None:
                wv_ += W(("sDVE", last_cast))
            dma_v = pr.emit("sync", lambda e, srcw=srcw: e.dma_start(out=xbufP[:, :, :], in_=srcw),
                            waits=wv_, inc="qs", amt=16)
            last_cast = pr.emit("vector", lambda e, mi=mi, half=half: e.tensor_copy(
                out=wres[:, mi, :, half * S:(half + 1) * S], in_=xbufP[:, :, :]),
                waits=W(("qs", dma_v), ("sPE", t_r[ilast])), inc="sDVE")
    bwprep_v = last_cast

    xh_v = pr.emit("sync", lambda e: e.dma_start(out=y2e[:, :, :], in_=xTr[:, :, 0:NE]),
                   waits=W(("sDVE", bwprep_v)), inc="qs", amt=16)
    pr.emit("sync", lambda e: e.dma_start(out=mrow_b[:, 0:NE], in_=stats_dram[0:1, 0:NE]),
            waits=W(("qs", statdma_done[0])), inc="qs", amt=16)
    erow_v = pr.emit("sync", lambda e: e.dma_start(out=Rrow_b[:, 0:NE], in_=stats_dram[1:2, 0:NE]), inc="qs", amt=16)

    pr.emit("tensor", lambda e: e.matmul(psBCM[:, 0:NE], lhsT=onesbc[:], rhs=mrow_b[:, 0:NE], start=True, stop=True),
            waits=W(("qs", erow_v), ("sACT", scb_v)), inc="sPE")
    ebc_v = pr.emit("tensor", lambda e: e.matmul(psBCR[:, 0:NE], lhsT=onesbc[:], rhs=Rrow_b[:, 0:NE], start=True, stop=True),
                    inc="sPE")
    for dd in range(DD):
        wv_ = W(("sPE", ebc_v), ("qs", xh_v)) if dd == 0 else W(("sDVE", pr.val("sDVE")))
        pr.emit("vector", lambda e, dd=dd: e.tensor_tensor(out=y2e[:, dd, :], in0=y2e[:, dd, :], in1=psBCM[:, 0:NE], op=ALU.subtract),
                waits=wv_, inc="sDVE")
        pr.emit("vector", lambda e, dd=dd: e.tensor_tensor(out=y2e[:, dd, :], in0=y2e[:, dd, :], in1=psBCR[:, 0:NE], op=ALU.mult),
                waits=W(("sDVE", pr.val("sDVE"))), inc="sDVE")
        pr.emit("vector", lambda e, dd=dd: e.tensor_tensor(out=y2e[:, dd, :], in0=y2e[:, dd, :], in1=ge_v[:, dd, :], op=ALU.mult),
                waits=W(("sDVE", pr.val("sDVE"))), inc="sDVE")
        pr.emit("vector", lambda e, dd=dd: e.tensor_tensor(out=y2e[:, dd, :], in0=y2e[:, dd, :], in1=be_v[:, dd, :], op=ALU.add),
                waits=W(("sDVE", pr.val("sDVE"))), inc="sDVE")
    y2e_v = pr.val("sDVE")
    pr.emit("sync", lambda e: e.dma_start(out=he_tm[:], in_=b1e_in), waits=W(("sDVE", bwprep_v)), inc="qs", amt=16)
    ebias_v = pr.emit("sync", lambda e: e.dma_start(out=fe_tm[:], in_=b2e_in), inc="qs", amt=16)

    if dbg_stop == "prep":
        dbg_finish()
        return _finish_build(nc, pr, blk, es, es2)

    psK_free = [("sDVE", t_z0[NSG - 2]), ("sDVE", t_z0[NSG - 1])]
    psV_free = [("sACT", t_evV[NSG - 2]), ("sACT", t_evV[NSG - 1])]
    psR_free = ("sACT", t_evR[NSG - 1])
    psStat_free = ("sACT", hev)

    exp_dma = [None] * 64
    exp_mm = [None] * 64
    exp_acc = [None] * 64
    exp_state = {"next": 0, "lhs2_ready": None}

    def emit_expert_chunks(nchunks):
        for _ in range(nchunks):
            ci = exp_state["next"]
            if ci >= 64:
                return
            li, rem = divmod(ci, 32)
            n, jc = divmod(rem, 2)
            if li == 1 and exp_state["lhs2_ready"] is None:
                return
            exp_state["next"] = ci + 1
            wsrc = w1e if li == 0 else w2e
            sgb = stg[ci % 2]
            src = wsrc[n].rearrange("(a p) o -> p a o", p=P)[:, :, jc * S:(jc + 1) * S]
            wv_ = [("qs", ebias_v)]
            if ci >= 2:
                wv_.append(("sPE", exp_mm[ci - 2]))
            exp_dma[ci] = pr.emit("scalar", lambda e, src=src, sgb=sgb: e.dma_start(out=sgb[:, :, :], in_=src),
                                  waits=wv_, inc="qe", amt=16)
            lhs = y2e if li == 0 else he_fm
            lhs_w = ("sDVE", y2e_v) if li == 0 else exp_state["lhs2_ready"]
            for dd in range(DD):
                wv_ = []
                if dd == 0:
                    wv_ = [("qe", exp_dma[ci]), lhs_w, psStat_free]
                    if ci >= 1:
                        wv_.append(("sDVE", exp_acc[ci - 1]))
                exp_mm[ci] = pr.emit("tensor", lambda e, dd=dd, lhs=lhs, sgb=sgb: e.matmul(
                    psStat[:, :], lhsT=lhs[:, dd, :], rhs=sgb[:, dd, :],
                    start=(dd == 0), stop=(dd == DD - 1)),
                    waits=wv_, inc="sPE" if dd == DD - 1 else None)
            dstv = he_tm if li == 0 else fe_tm
            exp_acc[ci] = pr.emit("vector", lambda e, n=n, jc=jc, dstv=dstv: e.scalar_tensor_tensor(
                out=dstv[:, jc * S:(jc + 1) * S], in0=psStat[:, :], scalar=id16[:, n:n + 1],
                in1=dstv[:, jc * S:(jc + 1) * S], op0=ALU.mult, op1=ALU.add),
                waits=W(("sPE", exp_mm[ci])), inc="sDVE")

    def emit_expert_l1_finish():
        gv = pr.emit("scalar", lambda e: e.activation(out=he_tm[:], in_=he_tm[:], func=AF.Gelu),
                     waits=W(("sDVE", exp_acc[31])), inc="sACT")
        last_ev = None
        for dd in range(DD):
            wv_ = [("sACT", gv)]
            if dd == 0:
                wv_.append(psV_free[0])
            tv = pr.emit("tensor", lambda e, dd=dd: e.transpose(
                psV[0][:, 0:16], in_=he_tm[:, dd * P:(dd + 1) * P], identity=id16[:]),
                waits=wv_, inc="sPE")
            last_ev = pr.emit("scalar", lambda e, dd=dd: e.activation(out=he_fm[:, dd, :], in_=psV[0][:, 0:16], func=AF.Copy),
                              waits=W(("sPE", tv)), inc="sACT")
            if dd < DD - 1:
                pr.streams["tensor"].append(lambda e, lv=last_ev: e.wait_ge(pr.sem["sACT"], lv))
        exp_state["lhs2_ready"] = ("sACT", last_ev)
        psV_free[0] = ("sACT", last_ev)

    def emit_expert_l2_finish():
        last_ev = None
        for dd in range(DD):
            wv_ = [("sDVE", exp_acc[63])]
            if dd == 0:
                wv_.append(psV_free[0])
            tv = pr.emit("tensor", lambda e, dd=dd: e.transpose(
                psV[0][:, 0:16], in_=fe_tm[:, dd * P:(dd + 1) * P], identity=id16[:]),
                waits=wv_, inc="sPE")
            last_ev = pr.emit("scalar", lambda e, dd=dd: e.activation(out=fe_fm[:, dd, :], in_=psV[0][:, 0:16], func=AF.Copy),
                              waits=W(("sPE", tv)), inc="sACT")
            if dd < DD - 1:
                pr.streams["tensor"].append(lambda e, lv=last_ev: e.wait_ge(pr.sem["sACT"], lv))
        psV_free[0] = ("sACT", last_ev)
        return ("sACT", last_ev)

    y2tt_done = {}
    y2aff_done = {}
    w1_done = {}
    gelu_last = {}
    w2_done = {}
    silu_last = {}
    nrm_done = {}
    wo_done = {}
    comb_done = {}
    fe_silu = [None]
    fe_fm_ready = [None]

    tile_order = list(range(2, NB)) + [1, 0]
    pv = None
    bcB_prev = ebc_v
    for ti, tt in enumerate(tile_order):
        t0 = tt * SB
        wv_ = [("sDVE", bwprep_v)] if pv is None else [("sDVE", comb_done[pv])]
        if dbg and pv == 2:
            wv_.append(("qs", pr.val("qs")))
        xv_ = pr.emit("sync", lambda e, t0=t0: e.dma_start(out=xB[:, :, :], in_=xTr[:, :, t0:t0 + SB]),
                      waits=wv_, inc="qs", amt=16)
        pr.emit("sync", lambda e, t0=t0: e.dma_start(out=mrow_b[:, 0:SB], in_=stats_dram[0:1, t0:t0 + SB]),
                waits=[("sPE", bcB_prev)], inc="qs", amt=16)
        row_v = pr.emit("sync", lambda e, t0=t0: e.dma_start(out=Rrow_b[:, 0:SB], in_=stats_dram[1:2, t0:t0 + SB]),
                        inc="qs", amt=16)
        wv_ = [("qs", row_v)]
        wv_.append(("sDVE", y2e_v) if pv is None else ("sDVE", y2tt_done[pv]))
        pr.emit("tensor", lambda e: e.matmul(psBCM[:, 0:SB], lhsT=onesbc[:], rhs=mrow_b[:, 0:SB], start=True, stop=True),
                waits=wv_, inc="sPE")
        bcB_prev = pr.emit("tensor", lambda e: e.matmul(psBCR[:, 0:SB], lhsT=onesbc[:], rhs=Rrow_b[:, 0:SB], start=True, stop=True),
                           inc="sPE")

        for dd in range(DD):
            wv_ = ()
            if dd == 0:
                wv_ = W(("sPE", bcB_prev), ("qs", xv_))
            pr.emit("vector", lambda e, dd=dd: e.tensor_tensor(
                out=scr1B[:, :], in0=xB[:, dd, :], in1=psBCM[:, 0:SB], op=ALU.subtract), waits=wv_, inc="sDVE")
            ttv = pr.emit("vector", lambda e, dd=dd: e.tensor_tensor(
                out=scr2B[:, :], in0=scr1B[:, :], in1=psBCR[:, 0:SB], op=ALU.mult),
                waits=W(("sDVE", pr.val("sDVE"))), inc="sDVE")
            wv_ = [("sDVE", ttv)]
            if dd == 0 and pv is not None:
                wv_.append(("sPE", w1_done[pv]))
            pr.emit("scalar", lambda e, dd=dd: e.activation(
                out=y2B[:, dd, :], in_=scr2B[:, :], func=AF.Identity,
                scale=small[:, C_GS + dd:C_GS + dd + 1], bias=small[:, C_BS + dd:C_BS + dd + 1]),
                waits=wv_, inc="sACT")
        y2tt_done[tt] = pr.val("sDVE")
        y2aff_done[tt] = pr.val("sACT")

        for hh in range(DD):
            pk = psK[hh % 2]
            for dd in range(DD):
                wv_ = []
                if dd == 0:
                    wv_ = [("sACT", y2aff_done[tt]), psK_free[hh % 2]]
                w1_done[tt] = pr.emit("tensor", lambda e, dd=dd, hh=hh, pk=pk: e.matmul(
                    pk[:, 0:SB], lhsT=wres[:, 1, dd, hh * P:(hh + 1) * P], rhs=y2B[:, dd, :],
                    start=(dd == 0), stop=(dd == DD - 1)),
                    waits=wv_, inc="sPE" if dd == DD - 1 else None)
            wv_ = [("sPE", w1_done[tt])]
            if hh == 0 and pv is not None:
                wv_.append(("sPE", w2_done[pv]))
            gelu_last[tt] = pr.emit("scalar", lambda e, hh=hh, pk=pk: e.activation(
                out=h1B[:, hh, :], in_=pk[:, 0:SB], func=AF.Gelu,
                bias=small[:, C_B1S + hh:C_B1S + hh + 1]),
                waits=wv_, inc="sACT")
            psK_free[hh % 2] = ("sACT", gelu_last[tt])

        for kk in range(DD):
            pvps = psV[kk % 2]
            for hh in range(DD):
                wv_ = []
                if hh == 0:
                    wv_ = [("sACT", gelu_last[tt]), psV_free[kk % 2]]
                w2_done[tt] = pr.emit("tensor", lambda e, hh=hh, kk=kk, pvps=pvps: e.matmul(
                    pvps[:, 0:SB], lhsT=wres[:, 2, hh, kk * P:(kk + 1) * P], rhs=h1B[:, hh, :],
                    start=(hh == 0), stop=(hh == DD - 1)),
                    waits=wv_, inc="sPE" if hh == DD - 1 else None)
            wv_ = [("sPE", w2_done[tt])]
            if kk == 0 and pv is not None:
                wv_.append(("sDVE", comb_done[pv]))
            silu_last[tt] = pr.emit("scalar", lambda e, kk=kk, pvps=pvps: e.activation(
                out=gB[:, kk, :], in_=pvps[:, 0:SB], func=AF.Silu,
                bias=small[:, C_B2S + kk:C_B2S + kk + 1]),
                waits=wv_, inc="sACT")
            psV_free[kk % 2] = ("sACT", silu_last[tt])

        if tt == 0:
            for dd in range(DD):
                fe_silu[0] = pr.emit("scalar", lambda e, dd=dd: e.activation(
                    out=gB[:, dd, 0:NE], in_=fe_fm[:, dd, :], func=AF.Silu),
                    waits=W(fe_fm_ready[0]) if dd == 0 else (), inc="sACT")

        for g in range(G):
            wv_ = ()
            if g == 0:
                wv_ = [("sDVE", gn_v)]
                if pv is not None:
                    wv_.append(("sPE", wo_done[pv]))
            nrm_done[tt] = pr.emit("vector", lambda e, g=g, t0=t0: e.tensor_scalar(
                out=nrmB[:, g, :], in0=rwkv[:, g, t0:t0 + SB],
                scalar1=sc_ch[:, g:g + 1], scalar2=bi_ch[:, g:g + 1], op0=ALU.mult, op1=ALU.add),
                waits=wv_, inc="sDVE")

        if dbg and tt == 2:
            pr.emit("sync", lambda e: e.dma_start(out=dbg_scan[:, 0:SB].bitcast(BF16)[:, 0:2 * SB],
                    in_=y2B.rearrange("p a t -> p (a t)")[:, 0:2 * SB]),
                    waits=W(("sACT", y2aff_done[tt])), inc="qs", amt=16)
            pr.emit("sync", lambda e: e.dma_start(out=dbg_scan[:, SB:2 * SB].bitcast(BF16)[:, 0:2 * SB],
                    in_=h1B.rearrange("p a t -> p (a t)")[:, 0:2 * SB]),
                    waits=W(("sACT", gelu_last[tt])), inc="qs", amt=16)
            pr.emit("sync", lambda e: e.dma_start(out=dbg_scan[:, 2 * SB:3 * SB].bitcast(BF16)[:, 0:2 * SB],
                    in_=nrmB.rearrange("p a t -> p (a t)")[:, 0:2 * SB]),
                    waits=W(("sDVE", nrm_done[tt])), inc="qs", amt=16)
            gsil_v = pr.emit("sync", lambda e: e.dma_start(out=dbg_scan[:, 3 * SB:3 * SB + 8 * SB],
                    in_=gB.rearrange("p a t -> p (a t)")),
                    waits=W(("sACT", silu_last[tt])), inc="qs", amt=16)
        sg_prev = None
        for oo in range(DD):
            for g in range(G):
                wv_ = []
                if g == 0:
                    wv_ = [("sDVE", nrm_done[tt])]
                    if oo == 0:
                        wv_.append(psR_free)
                    else:
                        wv_.append(("sDVE", sg_prev))
                wo_done[tt] = pr.emit("tensor", lambda e, g=g, oo=oo: e.matmul(
                    psR[:, 0:SB], lhsT=wres[:, 0, g, oo * P:(oo + 1) * P], rhs=nrmB[:, g, :],
                    start=(g == 0), stop=(g == G - 1)),
                    waits=wv_, inc="sPE" if g == G - 1 else None)
            wv_ = [("sPE", wo_done[tt]), ("sACT", silu_last[tt])]
            if dbg and tt == 2 and oo == 0:
                wv_.append(("qs", gsil_v))
            if tt == 0:
                wv_.append(("sACT", fe_silu[0]))
            pr.emit("vector", lambda e, oo=oo: e.scalar_tensor_tensor(
                out=gB[:, oo, :], in0=gB[:, oo, :], scalar=1.0, in1=psR[:, 0:SB], op0=ALU.mult, op1=ALU.mult),
                waits=wv_, inc="sDVE")
            sg_prev = pr.val("sDVE")
            pr.emit("vector", lambda e, oo=oo: e.tensor_tensor(
                out=xB[:, oo, :], in0=xB[:, oo, :], in1=gB[:, oo, :], op=ALU.add),
                waits=W(("sDVE", sg_prev)), inc="sDVE")
        comb_done[tt] = pr.val("sDVE")
        if dbg and tt == 2:
            pr.emit("sync", lambda e: e.dma_start(out=dbg_scan[:, 11 * SB:19 * SB],
                    in_=xB.rearrange("p a t -> p (a t)")),
                    waits=W(("sDVE", comb_done[tt])), inc="qs", amt=16)
            pr.emit("sync", lambda e: e.dma_start(out=dbg_scan[:, 19 * SB:27 * SB],
                    in_=gB.rearrange("p a t -> p (a t)")), inc="qs", amt=16)
        psR_free = ("sDVE", sg_prev)

        pr.emit("sync", lambda e, t0=t0: e.dma_start(out=outTr[:, :, t0:t0 + SB], in_=xB[:, :, :]),
                waits=W(("sDVE", comb_done[tt])), inc="qs", amt=16)

        if ti <= 4:
            emit_expert_chunks(7)
            if ti == 4:
                emit_expert_l1_finish()
        elif ti <= 13:
            emit_expert_chunks(4)
            if ti == 13:
                emit_expert_chunks(64)
                fe_fm_ready[0] = emit_expert_l2_finish()
        pv = tt

    pr.emit("sync", lambda e: None, waits=W(("qs", pr.val("qs")), ("qe", pr.val("qe"))))

    import sys as _sys
    if Prog.ser_rng:
        print(f"[dbg] serialized {Prog.ser_count} DVE ops of {len(pr.streams['vector'])}", file=_sys.stderr)
    return _finish_build(nc, pr, blk, es, es2)


def _finish_build(nc, pr, blk, es, es2):
    import sys as _sys
    print(f"[dbg] ser_rng={Prog.ser_rng} serialized {Prog.ser_count} of {len(pr.streams['vector'])} DVE ops", file=_sys.stderr)
    pr.check_deadlock()

    @blk.sync
    def _(eng):
        for f in pr.streams["sync"]:
            f(eng)

    @blk.tensor
    def _(eng):
        for f in pr.streams["tensor"]:
            f(eng)

    @blk.scalar
    def _(eng):
        for f in pr.streams["scalar"]:
            f(eng)

    @blk.vector
    def _(eng):
        for f in pr.streams["vector"]:
            f(eng)

    @blk.gpsimd
    def _(eng):
        for f in pr.streams["gpsimd"]:
            f(eng)

    es2.close()
    es.close()
    return nc


_NC_CACHE = {}


def _host_prep(inputs):
    f32c = lambda a: np.ascontiguousarray(np.asarray(a, dtype=np.float32))
    x = f32c(inputs["L0_attn_x"])
    shared = {}
    shared["wrT"] = f32c(np.asarray(inputs["L0_attn_W_r"], np.float32).T)
    shared["wkT"] = f32c(np.asarray(inputs["L0_attn_W_k"], np.float32).T)
    shared["wvT"] = f32c(np.asarray(inputs["L0_attn_W_v"], np.float32).T)
    shared["woT"] = f32c(np.asarray(inputs["L0_attn_W_o"], np.float32).T)
    shared["w1s"] = f32c(inputs["L0_ffn_W1_s"])
    shared["w2s"] = f32c(inputs["L0_ffn_W2_s"])
    shared["w1e"] = f32c(inputs["L0_ffn_W1_e"])
    shared["w2e"] = f32c(inputs["L0_ffn_W2_e"])
    shared["b1e"] = f32c(inputs["L0_ffn_b1_e"])
    shared["b2e"] = f32c(inputs["L0_ffn_b2_e"])

    sm = np.zeros((P, SM_COLS), np.float32)
    def put8(col, vec):
        sm[:, col:col + 8] = np.asarray(vec, np.float32).reshape(8, P).T
    put8(C_TMR, np.asarray(inputs["L0_attn_time_mix_r"]).reshape(1024))
    put8(C_TMK, np.asarray(inputs["L0_attn_time_mix_k"]).reshape(1024))
    put8(C_TMV, np.asarray(inputs["L0_attn_time_mix_v"]).reshape(1024))
    put8(C_TD, np.asarray(inputs["L0_attn_time_decay"]).reshape(1024))
    put8(C_TF, np.asarray(inputs["L0_attn_time_first"]).reshape(1024))
    put8(C_LNG, np.asarray(inputs["L0_attn_ln_gamma"]).reshape(1024))
    put8(C_LNB, np.asarray(inputs["L0_attn_ln_beta"]).reshape(1024))
    put8(C_GS, np.asarray(inputs["L0_ffn_g_s"]).reshape(1024))
    put8(C_BS, np.asarray(inputs["L0_ffn_b_s"]).reshape(1024))
    put8(C_B1S, np.asarray(inputs["L0_ffn_b1_s"]).reshape(1024))
    put8(C_B2S, np.asarray(inputs["L0_ffn_b2_s"]).reshape(1024))
    ge = np.asarray(inputs["L0_ffn_g_e"], np.float32)
    be = np.asarray(inputs["L0_ffn_b_e"], np.float32)
    sm[:, C_GE:C_GE + 128] = ge.T.reshape(8, P, 16).transpose(1, 0, 2).reshape(P, 128)
    sm[:, C_BE:C_BE + 128] = be.T.reshape(8, P, 16).transpose(1, 0, 2).reshape(P, 128)
    sm[:, C_CONST + 0] = 1.0 / 1024.0
    sm[:, C_CONST + 1] = D0
    sm[:, C_CONST + 2] = 0.0
    sm[:, C_CONST + 3] = 1.0
    sm[:, C_CONST + 4] = EPS
    sm[:, C_CONST + 5] = -0.5
    shared["smallpack"] = sm
    shared["ones_bc"] = np.ones((1, P), np.float32)
    shared["id16"] = np.eye(16, dtype=np.float32)
    mk = np.zeros((P, 2), np.float32)
    mk[:64, 0] = 1.0
    mk[64:, 1] = 1.0
    shared["mask64"] = mk
    shared["mask64T"] = f32c(mk.T)
    return x, shared


def kernel(**inputs):
    x, shared = _host_prep(inputs)
    if "nc" not in _NC_CACHE:
        _NC_CACHE["nc"] = build_nc()
    nc = _NC_CACHE["nc"]
    in_maps = []
    for b in range(8):
        m = dict(shared)
        m["xT"] = np.ascontiguousarray(x[b].T)
        in_maps.append(m)
    res = run_bass_kernel_spmd(nc, in_maps, core_ids=list(range(8)))
    out = np.empty((8, T, 1024), np.float32)
    for b in range(8):
        out[b] = res.results[b]["outT"].T
    return out
